# version 14
# baseline (speedup 1.0000x reference)
import sys
if "/opt/trn_rl_repo" not in sys.path:
    sys.path.insert(0, "/opt/trn_rl_repo")

import numpy as np
import ml_dtypes
from contextlib import ExitStack

import concourse.bass as bass
import concourse.mybir as mybir
from concourse.bass_utils import run_bass_kernel_spmd

F32 = mybir.dt.float32
BF16 = mybir.dt.bfloat16
AF = mybir.ActivationFunctionType
ALU = mybir.AluOpType
AX = mybir.AxisListType

NCORES = 8
B, S, H = 2, 4096, 4096
T = 1024
CH = 512
NTT = CH // 128
HB = H // 128
ITN = 86
I = 11008
HCN = H // 512
D = 1024
DT = D // 128
EPS = 1e-5

SUM, SSQ0, SSQ, MEAN, MSQ, VAR, LNV, RSTD, NB0, NBIAS = 0, 1, 9, 10, 11, 12, 13, 14, 15, 16
AMX, AMXS, ADEN, AR0, ANT, ADINV = 0, 1, 2, 3, 4, 5


class Sem:
    def __init__(self, h):
        self.h = h
        self.n = 0

    def inc(self, k=1):
        self.n += k
        return self.n


class Prog:
    ENGINES = ("sync", "gpsimd", "vector", "scalar", "tensor")

    def __init__(self):
        self.q = {e: [] for e in self.ENGINES}

    def emit(self, eng, fn):
        self.q[eng].append(fn)

    def wait(self, eng, sem, val):
        if val > 0:
            self.emit(eng, lambda e, s=sem.h, v=val: e.wait_ge(s, v))


def build_nc(n_sub=3, dbg=False):
    nc = bass.Bass("TRN2", target_bir_lowering=False)

    xin = nc.dram_tensor("xin", [T, H], F32, kind="ExternalInput").ap()
    wg_d = nc.dram_tensor("wg_d", [ITN, 128, HB * 128], BF16, kind="ExternalInput").ap()
    wu_d = nc.dram_tensor("wu_d", [ITN, 128, HB * 128], BF16, kind="ExternalInput").ap()
    wd_d = nc.dram_tensor("wd_d", [HCN, 43, 128, 1024], BF16, kind="ExternalInput").ap()
    wqk_d = nc.dram_tensor("wqk_d", [16, 128, HB * 128], F32, kind="ExternalInput").ap()
    wv_d = nc.dram_tensor("wv_d", [HB, 128, D], F32, kind="ExternalInput").ap()
    wo_d = nc.dram_tensor("wo_d", [HCN, 128, DT * 512], F32, kind="ExternalInput").ap()
    gs_d = nc.dram_tensor("gs_d", [128, ITN], F32, kind="ExternalInput").ap()
    us_d = nc.dram_tensor("us_d", [128, ITN], F32, kind="ExternalInput").ap()
    bqkv_d = nc.dram_tensor("bqkv_d", [128, 24], F32, kind="ExternalInput").ap()
    bqv_d = nc.dram_tensor("bqv_d", [1, D], F32, kind="ExternalInput").ap()
    bo_d = nc.dram_tensor("bo_d", [1, H], F32, kind="ExternalInput").ap()
    ds_d = nc.dram_tensor("ds_d", [1, H], F32, kind="ExternalInput").ap()
    id_d = nc.dram_tensor("id_d", [128, 128], BF16, kind="ExternalInput").ap()
    yout = nc.dram_tensor("yout", [T, H], F32, kind="ExternalOutput").ap()
    if dbg:
        dbg_stats = nc.dram_tensor("dbg_stats", [128, 224], F32, kind="ExternalOutput").ap()
        dbg_xlnT = nc.dram_tensor("dbg_xlnT", [128, HB * CH], BF16, kind="ExternalOutput").ap()
        dbg_inter = nc.dram_tensor("dbg_inter", [128, ITN * CH], BF16, kind="ExternalOutput").ap()
        dbg_stats2 = nc.dram_tensor("dbg_stats2", [128, 224], F32, kind="ExternalOutput").ap()
        dbg_xlnT2 = nc.dram_tensor("dbg_xlnT2", [128, HB * CH], BF16, kind="ExternalOutput").ap()
    if dbg == 2:
        dbg_r1 = nc.dram_tensor("dbg_r1", [T, H], F32, kind="ExternalOutput").ap()
        dbg_ccin = nc.dram_tensor("dbg_ccin", [2048, D], BF16, kind="ExternalOutput").ap()
        dbg_ccout = nc.dram_tensor("dbg_ccout", [8192, D], BF16, kind="ExternalOutput").ap()
        dbg_O = nc.dram_tensor("dbg_O", [128, 16384], BF16, kind="ExternalOutput").ap()
        dbg_astats = nc.dram_tensor("dbg_astats", [128, 224], F32, kind="ExternalOutput").ap()

    r1 = nc.dram_tensor("r1", [T, H], F32, kind="Internal").ap()
    r2 = nc.dram_tensor("r2", [T, H], F32, kind="Internal").ap()
    cc_in_t = nc.dram_tensor("cc_in", [2048, D], BF16, kind="Internal")
    cc_out_t = nc.dram_tensor("cc_out", [4 * 2048, D], BF16, kind="Internal")
    cc_in = cc_in_t.ap()
    cc_out = cc_out_t.ap()

    es = ExitStack()
    xlnT = es.enter_context(nc.sbuf_tensor("xlnT", [128, HB * CH], BF16))
    inter = es.enter_context(nc.sbuf_tensor("inter", [128, ITN * CH], BF16))
    wgu = es.enter_context(nc.sbuf_tensor("wgu", [128, 16384], BF16))
    wdb = es.enter_context(nc.sbuf_tensor("wdb", [128, 3072], BF16))
    lnsc = es.enter_context(nc.sbuf_tensor("lnsc", [128, 1024], BF16))
    srcln = es.enter_context(nc.sbuf_tensor("srcln", [128, 8192], F32))
    srcadd = es.enter_context(nc.sbuf_tensor("srcadd", [128, 1024], F32))
    dstst = es.enter_context(nc.sbuf_tensor("dstst", [128, 1024], F32))
    evtmp = es.enter_context(nc.sbuf_tensor("evtmp", [128, 512], F32))
    bcast = es.enter_context(nc.sbuf_tensor("bcast", [128, 512], F32))
    silu_s = es.enter_context(nc.sbuf_tensor("silu_s", [128, 1024], BF16))
    stats = es.enter_context(nc.sbuf_tensor("stats", [128, 192], F32))
    cst = es.enter_context(nc.sbuf_tensor("cst", [128, 4], F32))
    gs_s = es.enter_context(nc.sbuf_tensor("gs_s", [128, ITN], F32))
    us_s = es.enter_context(nc.sbuf_tensor("us_s", [128, ITN], F32))
    bqkv_s = es.enter_context(nc.sbuf_tensor("bqkv_s", [128, 24], F32))
    id_s = es.enter_context(nc.sbuf_tensor("id_s", [128, 128], BF16))

    pstr = [es.enter_context(nc.psum_tensor("pstr%d" % i, [128, 1024], BF16)) for i in range(2)]
    psf = [es.enter_context(nc.psum_tensor("psf%d" % i, [128, 512], F32)) for i in range(6)]

    sWL = Sem(es.enter_context(nc.semaphore("sWL")))
    sGL = Sem(es.enter_context(nc.semaphore("sGL")))
    sST = Sem(es.enter_context(nc.semaphore("sST")))
    semI = Sem(es.enter_context(nc.semaphore("semI")))
    semWgu = [Sem(es.enter_context(nc.semaphore("semWgu%d" % i))) for i in range(2)]
    semWd = [Sem(es.enter_context(nc.semaphore("semWd%d" % i))) for i in range(3)]
    semSrcln = [Sem(es.enter_context(nc.semaphore("semSrcln%d" % i))) for i in range(2)]
    semSrcadd = [Sem(es.enter_context(nc.semaphore("semSrcadd%d" % i))) for i in range(2)]
    semBcG = Sem(es.enter_context(nc.semaphore("semBcG")))
    semBcS = Sem(es.enter_context(nc.semaphore("semBcS")))
    semWguG = [Sem(es.enter_context(nc.semaphore("semWguG%d" % i))) for i in range(2)]
    semWdG = [Sem(es.enter_context(nc.semaphore("semWdG%d" % i))) for i in range(3)]
    semK = Sem(es.enter_context(nc.semaphore("semK")))
    semSt = [Sem(es.enter_context(nc.semaphore("semSt%d" % i))) for i in range(2)]
    semSilu = [Sem(es.enter_context(nc.semaphore("semSilu%d" % i))) for i in range(2)]
    sCC = Sem(es.enter_context(nc.semaphore("sCC")))
    sPE = Sem(es.enter_context(nc.semaphore("sPE")))
    sDV = Sem(es.enter_context(nc.semaphore("sDV")))
    sAC = Sem(es.enter_context(nc.semaphore("sAC")))
    sAC2 = Sem(es.enter_context(nc.semaphore("sAC2")))

    P = Prog()

    def dma(eng, sem, out_ap, in_ap):
        v = sem.inc(16)
        P.emit(eng, lambda e, o=out_ap, i=in_ap, s=sem.h: e.dma_start(out=o, in_=i).then_inc(s, 16))
        return v

    def mm(out_ap, lhsT_ap, rhs_ap, start, stop, inc=False):
        if inc:
            v = sPE.inc(1)
            P.emit("tensor", lambda e, o=out_ap, l=lhsT_ap, r=rhs_ap, a=start, b=stop, s=sPE.h:
                   e.matmul(o, l, r, start=a, stop=b).then_inc(s, 1))
            return v
        P.emit("tensor", lambda e, o=out_ap, l=lhsT_ap, r=rhs_ap, a=start, b=stop:
               e.matmul(o, l, r, start=a, stop=b))
        return sPE.n

    def trans(out_ap, in_ap, inc=False):
        if inc:
            v = sPE.inc(1)
            P.emit("tensor", lambda e, o=out_ap, i=in_ap, s=sPE.h:
                   e.transpose(o, i, id_s[:, :]).then_inc(s, 1))
            return v
        P.emit("tensor", lambda e, o=out_ap, i=in_ap: e.transpose(o, i, id_s[:, :]))
        return sPE.n

    def pe_flush():
        v = sPE.inc(1)
        P.emit("tensor", lambda e, s=sPE.h:
               e.matmul(psf[5][:, 0:128], id_s[:, :], id_s[:, 0:128],
                        start=True, stop=True).then_inc(s, 1))
        return v

    def dve(fn_name, *aps, **kw):
        v = sDV.inc(1)
        P.emit("vector", lambda e, n=fn_name, a=aps, k=kw, s=sDV.h:
               getattr(e, n)(*a, **k).then_inc(s, 1))
        return v

    def act(fn_name, *aps, **kw):
        v = sAC.inc(1)
        P.emit("scalar", lambda e, n=fn_name, a=aps, k=kw, s=sAC.h:
               getattr(e, n)(*a, **k).then_inc(s, 1))
        return v

    dma("sync", semI, gs_s[:, :], gs_d)
    dma("sync", semI, us_s[:, :], us_d)
    dma("sync", semI, bqkv_s[:, :], bqkv_d)
    init_loads = dma("sync", semI, id_s[:, :], id_d)
    P.wait("tensor", semI, init_loads)
    P.wait("scalar", semI, init_loads)
    P.wait("vector", semI, init_loads)
    P.emit("vector", lambda e: e.memset(cst[:, 0:1], EPS))
    P.emit("vector", lambda e: e.memset(cst[:, 1:2], -0.5))
    cst_dv = sDV.inc(1)
    P.emit("vector", lambda e, s=sDV.h: e.memset(cst[:, 2:3], 1.0 / 32.0).then_inc(s, 1))
    P.wait("scalar", sDV, cst_dv)

    st = {
        "xlnT_pe_done": 0,
        "inter_pe_done": 0,
        "wgu_pe_done": 0,
        "wd_pe_done": 0,
        "lnsc_pe": [0, 0],
        "pstr_ac": [0, 0],
        "psf_free_dv": [0] * 6,
        "psf_free_ac": [0] * 6,
        "srcln_free": [(None, 0), (None, 0)],
        "silu_free": [(None, 0), (None, 0)],
        "wgu_par_pe": [0, 0],
        "wd_par_pe": [0, 0, 0],
        "srcadd_free": [(None, 0), (None, 0)],
        "dstst_free": [(None, 0), (None, 0)],
        "bcast_dv": 0,
        "evtmp_dv": 0,
        "stores_all": 0,
    }

    def wtup(eng, tup):
        sem, val = tup
        if sem is not None and val > 0:
            P.wait(eng, sem, val)

    def ln_chunk(src, cc):
        for tt in range(NTT):
            R = cc * CH + tt * 128
            par = tt % 2
            sl = srcln[:, par * 4096:(par + 1) * 4096]
            so = tt * 32
            wtup("gpsimd", st["srcln_free"][par])
            ld = dma("gpsimd", semSrcln[par], sl, src[R:R + 128, :])
            P.wait("vector", semSrcln[par], ld)
            dve("reduce_sum", stats[:, so + SUM:so + SUM + 1], sl, axis=AX.X)
            P.wait("scalar", semSrcln[par], ld)
            sq_last = 0
            sq_prev = [0, 0]
            for q in range(8):
                qh = lnsc[:, (q % 2) * 512:(q % 2 + 1) * 512]
                P.wait("scalar", sPE, st["lnsc_pe"][q % 2])
                P.wait("scalar", sAC, sq_prev[q % 2])
                sq_last = act("activation", qh, sl[:, q * 512:(q + 1) * 512], AF.Square,
                              accum_out=stats[:, so + SSQ0 + q:so + SSQ0 + q + 1])
                sq_prev[q % 2] = sq_last
            P.wait("vector", sAC, sq_last)
            v = dve("reduce_sum", stats[:, so + SSQ:so + SSQ + 1],
                    stats[:, so + SSQ0:so + SSQ0 + 8], axis=AX.X)
            P.wait("vector", sDV, v)
            v = dve("tensor_scalar_mul", stats[:, so + MEAN:so + MEAN + 1],
                    stats[:, so + SUM:so + SUM + 1], 1.0 / H)
            P.wait("vector", sDV, v)
            v = dve("tensor_mul", stats[:, so + MSQ:so + MSQ + 1],
                    stats[:, so + MEAN:so + MEAN + 1], stats[:, so + MEAN:so + MEAN + 1])
            P.wait("vector", sDV, v)
            v = dve("scalar_tensor_tensor", stats[:, so + VAR:so + VAR + 1],
                    stats[:, so + SSQ:so + SSQ + 1], 1.0 / H,
                    stats[:, so + MSQ:so + MSQ + 1], ALU.mult, ALU.subtract)
            P.wait("scalar", sDV, v)
            v = act("activation", stats[:, so + LNV:so + LNV + 1],
                    stats[:, so + VAR:so + VAR + 1], AF.Ln, bias=cst[:, 0:1])
            P.wait("scalar", sAC, v)
            v = act("activation", stats[:, so + RSTD:so + RSTD + 1],
                    stats[:, so + LNV:so + LNV + 1], AF.Exp, scale=cst[:, 1:2])
            P.wait("vector", sAC, v)
            v = dve("tensor_mul", stats[:, so + NB0:so + NB0 + 1],
                    stats[:, so + MEAN:so + MEAN + 1], stats[:, so + RSTD:so + RSTD + 1])
            P.wait("vector", sDV, v)
            nb_done = dve("tensor_scalar_mul", stats[:, so + NBIAS:so + NBIAS + 1],
                          stats[:, so + NB0:so + NB0 + 1], -1.0)
            last_dv = nb_done
            for q in range(8):
                qh = lnsc[:, (q % 2) * 512:(q % 2 + 1) * 512]
                P.wait("vector", sDV, last_dv)
                P.wait("vector", sPE, st["lnsc_pe"][q % 2])
                nv = dve("tensor_scalar", qh, sl[:, q * 512:(q + 1) * 512],
                         stats[:, so + RSTD:so + RSTD + 1],
                         stats[:, so + NBIAS:so + NBIAS + 1], ALU.mult, ALU.add)
                bank = q % 2
                last_oct = (q == 7 and tt == NTT - 1)
                P.wait("tensor", sDV, nv if last_oct else nv + 1)
                P.wait("tensor", sAC, st["pstr_ac"][bank])
                for k in range(4):
                    trans(pstr[bank][:, k * 128:(k + 1) * 128],
                          qh[:, k * 128:(k + 1) * 128])
                pe_v = pe_flush()
                st["lnsc_pe"][q % 2] = pe_v
                dest = xlnT[:, :].rearrange("p (hb t) -> p hb t", t=CH)[:, 4 * q:4 * q + 4,
                                                                       tt * 128:(tt + 1) * 128]
                srcv = pstr[bank][:, 0:512].rearrange("p (k t) -> p k t", t=128)
                P.wait("scalar", sPE, pe_v)
                ev = act("copy", dest, srcv)
                st["pstr_ac"][bank] = ev
            st["srcln_free"][par] = (sDV, sDV.n)
        return sAC.n

    def gateup_chunk(xln_ready_ac):
        for it in range(ITN):
            par = it % 2
            wslab = wgu[:, par * 8192:par * 8192 + 8192]
            P.wait("sync", sPE, st["wgu_par_pe"][par])
            lwg = dma("sync", semWgu[par], wslab[:, 0:4096], wg_d[it])
            lwu = dma("sync", semWgu[par], wslab[:, 4096:8192], wu_d[it])
            pg, pu = psf[2 * par], psf[2 * par + 1]
            pgi, pui = 2 * par, 2 * par + 1
            P.wait("tensor", semWgu[par], lwu)
            if it < 2:
                P.wait("tensor", sAC, xln_ready_ac)
            P.wait("tensor", sAC, max(st["psf_free_ac"][pgi], st["psf_free_ac"][pui]))
            P.wait("tensor", sDV, max(st["psf_free_dv"][pgi], st["psf_free_dv"][pui]))
            for hb in range(HB):
                mm(pg[:, :], wslab[:, hb * 128:(hb + 1) * 128],
                   xlnT[:, hb * CH:(hb + 1) * CH], start=(hb == 0), stop=(hb == HB - 1))
            for hb in range(HB):
                mm(pu[:, :], wslab[:, 4096 + hb * 128:4096 + (hb + 1) * 128],
                   xlnT[:, hb * CH:(hb + 1) * CH], start=(hb == 0), stop=(hb == HB - 1))
            pe_v = pe_flush()
            st["wgu_par_pe"][par] = pe_v
            sp = it % 2
            sh = silu_s[:, sp * 512:(sp + 1) * 512]
            P.wait("scalar", sPE, pe_v)
            wtup("scalar", st["silu_free"][sp])
            sv = act("activation", sh, pg[:, :], AF.Silu, scale=gs_s[:, it:it + 1])
            st["psf_free_ac"][pgi] = sv
            P.wait("vector", sAC, sv)
            iv = dve("scalar_tensor_tensor", inter[:, it * CH:(it + 1) * CH],
                     pu[:, :], us_s[:, it:it + 1], sh, ALU.mult, ALU.mult)
            st["psf_free_dv"][pui] = iv
            st["silu_free"][sp] = (sDV, iv)
        return sDV.n

    def down_chunk(src, dst, cc, inter_done_dv):
        first = True
        for hc in range(HCN):
            P.wait("gpsimd", sDV, st["bcast_dv"])
            bc = dma("gpsimd", semBcG, bcast[:, :],
                     ds_d[0:1, hc * 512:(hc + 1) * 512].broadcast_to([128, 512]))
            if first:
                P.wait("tensor", sDV, inter_done_dv)
                first = False
            for tt in range(NTT):
                P.wait("tensor", sDV, st["psf_free_dv"][tt])
                P.wait("tensor", sAC, st["psf_free_ac"][tt])
            hc_pe = 0
            for itp in range(43):
                par = itp % 3
                P.wait("sync", sPE, st["wd_par_pe"][par])
                lw = dma("sync", semWd[par], wdb[:, par * 1024:(par + 1) * 1024], wd_d[hc, itp])
                P.wait("tensor", semWd[par], lw)
                for k in range(2):
                    it = 2 * itp + k
                    for tt in range(NTT):
                        last = (k == 1) and (tt == NTT - 1)
                        pe_v = mm(psf[tt][:, :],
                                  inter[:, it * CH + tt * 128:it * CH + (tt + 1) * 128],
                                  wdb[:, par * 1024 + k * 512:par * 1024 + (k + 1) * 512],
                                  start=(it == 0), stop=(it == I // 128 - 1),
                                  inc=last)
                st["wd_par_pe"][par] = pe_v
            hc_pe = pe_flush()
            for tt in range(NTT):
                R = cc * CH + tt * 128
                u = hc * NTT + tt
                spar = u % 2
                wtup("gpsimd", st["srcadd_free"][spar])
                sa = dma("gpsimd", semSrcadd[spar], srcadd[:, spar * 512:(spar + 1) * 512],
                         src[R:R + 128, hc * 512:(hc + 1) * 512])
                P.wait("vector", sPE, hc_pe)
                P.wait("vector", semBcG, bc)
                P.wait("vector", sDV, st["evtmp_dv"])
                t1 = dve("tensor_tensor", evtmp[:, :], psf[tt][:, :], bcast[:, :], ALU.mult)
                st["bcast_dv"] = t1
                P.wait("vector", semSrcadd[spar], sa)
                wtup("vector", st["dstst_free"][spar])
                P.wait("vector", sDV, t1)
                rn = dve("scalar_tensor_tensor", dstst[:, spar * 512:(spar + 1) * 512],
                         evtmp[:, :], 0.5, srcadd[:, spar * 512:(spar + 1) * 512],
                         ALU.mult, ALU.add)
                st["evtmp_dv"] = rn
                st["psf_free_dv"][tt] = rn
                st["srcadd_free"][spar] = (sDV, rn)
                P.wait("gpsimd", sDV, rn)
                sv = dma("gpsimd", semSt[spar], dst[R:R + 128, hc * 512:(hc + 1) * 512],
                         dstst[:, spar * 512:(spar + 1) * 512])
                st["dstst_free"][spar] = (semSt[spar], sv)

    def ffn_phase(src, dst, dump=False):
        xr0 = ln_chunk(src, 0)
        gu0 = gateup_chunk(xr0)
        st["xlnT_pe_done"] = sPE.n
        xr1 = ln_chunk(src, 1)
        down_chunk(src, dst, 0, gu0)
        gu1 = gateup_chunk(xr1)
        st["xlnT_pe_done"] = sPE.n
        down_chunk(src, dst, 1, gu1)
        st["inter_pe_done"] = sPE.n

    def attn_phase(src, dst):
        QOFF = 8 * 4096
        q_evac_ac = 0
        for cc in range(2):
            xr = ln_chunk(src, cc)
            ft_order = list(range(8, 16)) + list(range(0, 8))
            for ft in ft_order[:8]:
                _qk_tile(ft, cc, xr, is_k=True)
            _v_chunk(cc, xr)
            for ft in ft_order[8:]:
                _qk_tile(ft, cc, xr, is_k=False)
            st["xlnT_pe_done"] = sPE.n
        q_evac_ac = sDV.n

        P.wait("gpsimd", semSilu[0], semSilu[0].n)
        P.wait("gpsimd", semSilu[1], semSilu[1].n)
        ccv = sCC.inc(1)
        P.emit("gpsimd", lambda e, s=sCC.h: e.collective_compute(
            "AllGather", ALU.bypass,
            replica_groups=[[0, 1, 2, 3], [4, 5, 6, 7]],
            ins=[cc_in_t.ap().opt()], outs=[cc_out_t.ap().opt()]).then_inc(s))

        if dbg == 2:
            P.wait("gpsimd", sCC, ccv)
            dma("gpsimd", sST, dbg_r1, src)
            dma("gpsimd", sST, dbg_ccin, cc_in)
            dma("gpsimd", sST, dbg_ccout, cc_out)
        P.wait("sync", sCC, ccv)
        P.wait("sync", sPE, st["inter_pe_done"])
        kload = 0
        for dt in range(DT):
            for rk in range(4):
                kload = dma("sync", semK,
                            inter[:, dt * 4096 + rk * 1024: dt * 4096 + (rk + 1) * 1024],
                            cc_out[rk * 2048 + dt * 128: rk * 2048 + (dt + 1) * 128, :])

        av_pe_done = 0
        for qg in range(4):
            for q2 in range(2):
                _scores(2 * qg + q2, kload, q_evac_ac)
            for q2 in range(2):
                _softmax_attnT(2 * qg + q2)
            _attn_v(qg)
        av_pe_done = sPE.n
        o_evac_done = sAC.n

        if dbg == 2:
            P.wait("gpsimd", sAC, o_evac_done)
            P.wait("gpsimd", sDV, sDV.n)
            dma("gpsimd", sST, dbg_O, xlnT[:, :])
            dma("gpsimd", sST, dbg_astats, stats[:, :])
        P.wait("tensor", sAC, o_evac_done)
        for qti in range(8):
            for dt in range(DT):
                bank = (qti * DT + dt) // 8 % 2
                slot = (qti * DT + dt) % 8
                if slot == 0:
                    P.wait("tensor", sAC, st["pstr_ac"][bank])
                trans(pstr[bank][:, slot * 128:(slot + 1) * 128],
                      xlnT[:, qti * 1024 + dt * 128: qti * 1024 + (dt + 1) * 128])
                if slot == 7:
                    pe_v = pe_flush()
                    base = (qti * DT + dt) - 7
                    P.wait("scalar", sPE, pe_v)
                    for j in range(8):
                        g = base + j
                        qq, dd = g // DT, g % DT
                        ev = act("copy",
                                 xlnT[:, 8192 + dd * 1024 + qq * 128: 8192 + dd * 1024 + (qq + 1) * 128],
                                 pstr[bank][:, j * 128:(j + 1) * 128])
                    st["pstr_ac"][bank] = ev
        ot_ac = sAC.n

        for hc in range(HCN):
            par = hc % 2
            wslab = wgu[:, par * 8192: par * 8192 + 4096]
            P.wait("gpsimd", sPE, max(av_pe_done, st["wgu_par_pe"][par]))
            lw = dma("gpsimd", semWguG[par], wslab, wo_d[hc])
            P.wait("sync", sDV, st["bcast_dv"])
            bc = dma("sync", semBcS, bcast[:, :],
                     bo_d[0:1, hc * 512:(hc + 1) * 512].broadcast_to([128, 512]))
            for qti in range(8):
                so = 128 + qti * 8
                bank = qti % 2
                P.wait("tensor", semWguG[par], lw)
                P.wait("tensor", sAC, ot_ac)
                P.wait("tensor", sDV, st["psf_free_dv"][bank])
                P.wait("tensor", sAC, st["psf_free_ac"][bank])
                for dt in range(DT):
                    mm(psf[bank][:, :],
                       xlnT[:, 8192 + dt * 1024 + qti * 128: 8192 + dt * 1024 + (qti + 1) * 128],
                       wslab[:, dt * 512:(dt + 1) * 512],
                       start=(dt == 0), stop=(dt == DT - 1))
                pe_v = pe_flush()
                st["wgu_par_pe"][par] = pe_v
                R = qti * 128
                u = hc * 8 + qti
                spar = u % 2
                wtup("gpsimd", st["srcadd_free"][spar])
                sa = dma("gpsimd", semSrcadd[spar], srcadd[:, spar * 512:(spar + 1) * 512],
                         src[R:R + 128, hc * 512:(hc + 1) * 512])
                P.wait("vector", sPE, pe_v)
                P.wait("vector", semBcS, bc)
                P.wait("vector", sDV, max(st["evtmp_dv"], st["dinv_dv"][qti]))
                t2 = dve("scalar_tensor_tensor", evtmp[:, :], psf[bank][:, :],
                         stats[:, so + ADINV:so + ADINV + 1], bcast[:, :],
                         ALU.mult, ALU.add)
                st["psf_free_dv"][bank] = t2
                st["bcast_dv"] = t2
                P.wait("vector", semSrcadd[spar], sa)
                wtup("vector", st["dstst_free"][spar])
                P.wait("vector", sDV, t2)
                rn = dve("tensor_add", dstst[:, spar * 512:(spar + 1) * 512],
                         evtmp[:, :], srcadd[:, spar * 512:(spar + 1) * 512])
                st["evtmp_dv"] = rn
                st["srcadd_free"][spar] = (sDV, rn)
                P.wait("gpsimd", sDV, rn)
                sv = dma("gpsimd", semSt[spar], dst[R:R + 128, hc * 512:(hc + 1) * 512],
                         dstst[:, spar * 512:(spar + 1) * 512])
                st["dstst_free"][spar] = (semSt[spar], sv)
        st["inter_pe_done"] = sPE.n

    def _qk_tile(ft, cc, xln_ready_ac, is_k):
        QOFF = 8 * 4096
        par = ft % 2
        wslab = wgu[:, par * 8192: par * 8192 + 4096]
        P.wait("gpsimd", sPE, st["wgu_par_pe"][par])
        lw = dma("gpsimd", semWguG[par], wslab, wqk_d[ft])
        bank = ft % 2
        pb = psf[bank]
        P.wait("tensor", semWguG[par], lw)
        P.wait("tensor", sAC, xln_ready_ac)
        P.wait("tensor", sAC, st["psf_free_ac"][bank])
        P.wait("tensor", sDV, st["psf_free_dv"][bank])
        for cb in range(HB):
            mm(pb[:, :], wslab[:, cb * 128:(cb + 1) * 128],
               xlnT[:, cb * CH:(cb + 1) * CH],
               start=(cb == 0), stop=(cb == HB - 1))
        pe_v = pe_flush()
        st["wgu_par_pe"][par] = pe_v
        P.wait("vector", sPE, pe_v)
        if is_k:
            dt = ft - 8
            sp = ft % 2
            sh = silu_s[:, sp * 512:(sp + 1) * 512]
            wtup("vector", st["silu_free"][sp])
            ev = dve("tensor_scalar_add", sh, pb[:, :], bqkv_s[:, ft:ft + 1])
            st["psf_free_dv"][bank] = ev
            P.wait("gpsimd", sDV, ev)
            sv = dma("gpsimd", semSilu[sp], cc_in[dt * 128:(dt + 1) * 128, cc * 512:(cc + 1) * 512], sh)
            st["silu_free"][sp] = (semSilu[sp], sv)
        else:
            dest = inter[:, QOFF + ft * 1024 + cc * 512: QOFF + ft * 1024 + (cc + 1) * 512]
            ev = dve("tensor_scalar_add", dest, pb[:, :], bqkv_s[:, ft:ft + 1])
            st["psf_free_dv"][bank] = ev

    def _v_chunk(cc, xln_ready_ac):
        for dc in range(2):
            P.wait("sync", sDV, st["bcast_dv"])
            bc = dma("sync", semBcS, bcast[:, :],
                     bqv_d[0:1, dc * 512:(dc + 1) * 512].broadcast_to([128, 512]))
            for tt in range(NTT):
                P.wait("tensor", sDV, st["psf_free_dv"][tt])
                P.wait("tensor", sAC, st["psf_free_ac"][tt])
            vv_pe = 0
            for cb in range(HB):
                par = cb % 3
                P.wait("gpsimd", sPE, st["wd_par_pe"][par])
                lw = dma("gpsimd", semWdG[par], wdb[:, par * 1024:(par + 1) * 1024], wv_d[cb])
                P.wait("tensor", semWdG[par], lw)
                if cb < 2:
                    P.wait("tensor", sAC, xln_ready_ac)
                for tt in range(NTT):
                    pe_v = mm(psf[tt][:, :],
                              xlnT[:, cb * CH + tt * 128: cb * CH + (tt + 1) * 128],
                              wdb[:, par * 1024 + dc * 512: par * 1024 + (dc + 1) * 512],
                              start=(cb == 0), stop=(cb == HB - 1), inc=(tt == NTT - 1))
                st["wd_par_pe"][par] = pe_v
            vv_pe = pe_flush()
            st["wd_par_pe"] = [vv_pe] * 3
            for tt in range(NTT):
                sp = tt % 2
                sh = silu_s[:, sp * 512:(sp + 1) * 512]
                P.wait("vector", sPE, vv_pe)
                P.wait("vector", semBcS, bc)
                wtup("vector", st["silu_free"][sp])
                ev = dve("tensor_tensor", sh, psf[tt][:, :], bcast[:, :], ALU.add)
                st["psf_free_dv"][tt] = ev
                st["bcast_dv"] = ev
                P.wait("gpsimd", sDV, ev)
                sv = dma("gpsimd", semSilu[sp],
                         cc_in[1024 + cc * 512 + tt * 128: 1024 + cc * 512 + (tt + 1) * 128,
                               dc * 512:(dc + 1) * 512], sh)
                st["silu_free"][sp] = (semSilu[sp], sv)

    def _scores(qti, kload, q_evac_ac):
        QOFF = 8 * 4096
        par = qti % 2
        sco = srcln[:, par * 4096:(par + 1) * 4096]
        ev_dv = 0
        for kc in range(8):
            bank = kc % 4
            P.wait("tensor", semK, kload)
            if qti == 0 and kc == 0:
                P.wait("tensor", sDV, q_evac_ac)
            P.wait("tensor", sDV, st["psf_free_dv"][bank])
            P.wait("tensor", sAC, st["psf_free_ac"][bank])
            for dt in range(DT):
                mm(psf[bank][:, :],
                   inter[:, QOFF + dt * 1024 + qti * 128: QOFF + dt * 1024 + (qti + 1) * 128],
                   inter[:, dt * 4096 + kc * 512: dt * 4096 + (kc + 1) * 512],
                   start=(dt == 0), stop=(dt == DT - 1))
            pe_v = pe_flush()
            P.wait("vector", sPE, pe_v)
            wtup("vector", st["srcln_free"][par] if kc == 0 else (None, 0))
            ev_dv = dve("tensor_copy", sco[:, kc * 512:(kc + 1) * 512], psf[bank][:, :])
            st["psf_free_dv"][bank] = ev_dv
        st["srcln_free"][par] = (sDV, ev_dv)
        st.setdefault("sco_dv", [0, 0])[par] = ev_dv

    def _softmax_attnT(qti):
        par = qti % 2
        sco = srcln[:, par * 4096:(par + 1) * 4096]
        so = 128 + qti * 8
        ev_dv = st["sco_dv"][par]
        P.wait("vector", sDV, ev_dv)
        v = dve("reduce_max", stats[:, so + AMX:so + AMX + 1], sco, axis=AX.X)
        P.wait("vector", sDV, v)
        v = dve("tensor_scalar_mul", stats[:, so + AMXS:so + AMXS + 1],
                stats[:, so + AMX:so + AMX + 1], -1.0 / 32.0)
        expd = wgu[:, 8192 + par * 4096: 8192 + (par + 1) * 4096]
        P.wait("scalar", sDV, v)
        P.wait("scalar", sPE, st.setdefault("expS_pe", [0, 0])[par])
        exv = act("activation", expd, sco, AF.Exp, scale=cst[:, 2:3],
                  bias=stats[:, so + AMXS:so + AMXS + 1],
                  accum_out=stats[:, so + ADEN:so + ADEN + 1])
        st["srcln_free"][par] = (sAC, exv)
        P.wait("vector", sAC, exv)
        v = dve("reciprocal", stats[:, so + AR0:so + AR0 + 1], stats[:, so + ADEN:so + ADEN + 1])
        P.wait("vector", sDV, v)
        v = dve("tensor_mul", stats[:, so + ANT:so + ANT + 1],
                stats[:, so + ADEN:so + ADEN + 1], stats[:, so + AR0:so + AR0 + 1])
        P.wait("vector", sDV, v)
        v = dve("tensor_scalar", stats[:, so + ANT:so + ANT + 1],
                stats[:, so + ANT:so + ANT + 1], -1.0, 2.0, ALU.mult, ALU.add)
        P.wait("vector", sDV, v)
        v = dve("tensor_mul", stats[:, so + ADINV:so + ADINV + 1],
                stats[:, so + AR0:so + AR0 + 1], stats[:, so + ANT:so + ANT + 1])
        st.setdefault("dinv_dv", [0] * 8)[qti] = v
        for g in range(4):
            bank = g % 2
            P.wait("tensor", sAC, max(exv, st["pstr_ac"][bank]))
            P.wait("tensor", sPE, st.setdefault("attnT_pe", 0))
            for k in range(8):
                kb = g * 8 + k
                trans(pstr[bank][:, k * 128:(k + 1) * 128],
                      expd[:, kb * 128:(kb + 1) * 128])
            pe_v = pe_flush()
            st["expS_pe"][par] = pe_v
            dest = wgu[:, :].rearrange("p (kb c) -> p kb c", c=256)[
                :, g * 8:(g + 1) * 8, (qti % 2) * 128:(qti % 2 + 1) * 128]
            srcv = pstr[bank][:, :].rearrange("p (k t) -> p k t", t=128)
            P.wait("scalar", sPE, pe_v)
            ev = act("copy", dest, srcv)
            st["pstr_ac"][bank] = ev
        st["attnT_ac"] = sAC.n

    def _attn_v(qg):
        for bank in range(4):
            P.wait("tensor", sAC, st["psf_free_ac"][bank])
            P.wait("tensor", sDV, st["psf_free_dv"][bank])
        P.wait("tensor", sAC, st["attnT_ac"])
        av_pe = 0
        for kb in range(HB):
            rk, lt = kb // 8, kb % 8
            par = kb % 3
            P.wait("sync", sCC, 1)
            P.wait("sync", sPE, st["wd_par_pe"][par])
            lw = dma("sync", semWd[par], wdb[:, par * 1024:(par + 1) * 1024],
                     cc_out[rk * 2048 + 1024 + lt * 128: rk * 2048 + 1024 + (lt + 1) * 128, :])
            P.wait("tensor", semWd[par], lw)
            for q2 in range(2):
                for dh in range(2):
                    pe_v = mm(psf[q2 * 2 + dh][:, :],
                              wgu[:, kb * 256 + q2 * 128: kb * 256 + (q2 + 1) * 128],
                              wdb[:, par * 1024 + dh * 512: par * 1024 + (dh + 1) * 512],
                              start=(kb == 0), stop=(kb == HB - 1),
                              inc=(q2 == 1 and dh == 1))
            st["wd_par_pe"][par] = pe_v
        av_pe = pe_flush()
        st["wd_par_pe"] = [av_pe] * 3
        st["attnT_pe"] = av_pe
        for q2 in range(2):
            for dh in range(2):
                qti = 2 * qg + q2
                P.wait("scalar", sPE, av_pe)
                P.wait("scalar", sPE, st["xlnT_pe_done"])
                ev = act("copy",
                         xlnT[:, qti * 1024 + dh * 512: qti * 1024 + (dh + 1) * 512],
                         psf[q2 * 2 + dh][:, :])
                st["psf_free_ac"][q2 * 2 + dh] = ev

    phases = []
    if n_sub == 1:
        phases = [("ffn", xin, yout)]
    elif n_sub == 2:
        phases = [("ffn", xin, r1), ("attn", r1, yout)]
    else:
        phases = [("ffn", xin, r1), ("attn", r1, r2), ("ffn", r2, yout)]

    for i, (kind, src, dst) in enumerate(phases):
        if i > 0:
            for eng in ("gpsimd", "sync"):
                P.wait(eng, semSt[0], semSt[0].n)
                P.wait(eng, semSt[1], semSt[1].n)
        if kind == "ffn":
            ffn_phase(src, dst, dump=(dbg and i == 0))
        else:
            attn_phase(src, dst)

    P.wait("gpsimd", semSt[0], semSt[0].n)
    P.wait("gpsimd", semSt[1], semSt[1].n)

    with nc.Block() as block:
        @block.sync
        def _(e):
            for f in P.q["sync"]:
                f(e)

        @block.gpsimd
        def _(e):
            for f in P.q["gpsimd"]:
                f(e)

        @block.vector
        def _(e):
            for f in P.q["vector"]:
                f(e)

        @block.scalar
        def _(e):
            for f in P.q["scalar"]:
                f(e)

        @block.tensor
        def _(e):
            for f in P.q["tensor"]:
                f(e)

    es.close()
    return nc


_BF = ml_dtypes.bfloat16


def _prep_weights(L0_attn_Wqkv, L0_attn_bqkv, L0_attn_Wo, L0_attn_bo,
                  L0_ffn_gate_wq, L0_ffn_up_wq, L0_ffn_down_wq,
                  L0_ffn_gate_scale, L0_ffn_up_scale, L0_ffn_down_scale):
    def qk_tiles(wq):
        A = np.asarray(wq).reshape(ITN, 128, HB, 128)
        A = np.ascontiguousarray(A.transpose(0, 3, 2, 1))
        return A.reshape(ITN, 128, HB * 128).astype(_BF)

    wg_h = qk_tiles(L0_ffn_gate_wq)
    wu_h = qk_tiles(L0_ffn_up_wq)
    Dq = np.asarray(L0_ffn_down_wq).reshape(HCN, 512, ITN, 128)
    Dq = np.ascontiguousarray(Dq.transpose(0, 2, 3, 1))
    wd_h = Dq.reshape(HCN, 43, 2, 128, 512).transpose(0, 1, 3, 2, 4)
    wd_h = np.ascontiguousarray(wd_h).reshape(HCN, 43, 128, 1024).astype(_BF)

    Wqkv = np.asarray(L0_attn_Wqkv, dtype=np.float32)
    A = Wqkv.reshape(HB, 128, 24, 128)
    wqk_h = np.ascontiguousarray(A[:, :, :16].transpose(2, 1, 0, 3)).reshape(16, 128, HB * 128)
    wv_h = np.ascontiguousarray(Wqkv[:, 2048:].reshape(HB, 128, D))
    Wo = np.asarray(L0_attn_Wo, dtype=np.float32)
    W = Wo.reshape(DT, 128, HCN, 512)
    wo_h = np.ascontiguousarray(W.transpose(2, 1, 0, 3)).reshape(HCN, 128, DT * 512)

    gs_h = np.ascontiguousarray(np.asarray(L0_ffn_gate_scale, np.float32).reshape(ITN, 128).T)
    us_h = np.ascontiguousarray(np.asarray(L0_ffn_up_scale, np.float32).reshape(ITN, 128).T)
    bq = np.asarray(L0_attn_bqkv, np.float32)
    bqkv_h = np.ascontiguousarray(bq.reshape(24, 128).T)
    bqv_h = np.ascontiguousarray(bq[2048:].reshape(1, D))
    bo_h = np.ascontiguousarray(np.asarray(L0_attn_bo, np.float32).reshape(1, H))
    ds_h = np.ascontiguousarray(np.asarray(L0_ffn_down_scale, np.float32).reshape(1, H))
    id_h = np.eye(128, dtype=_BF)
    return dict(wg_d=wg_h, wu_d=wu_h, wd_d=wd_h, wqk_d=wqk_h, wv_d=wv_h, wo_d=wo_h,
                gs_d=gs_h, us_d=us_h, bqkv_d=bqkv_h, bqv_d=bqv_h, bo_d=bo_h,
                ds_d=ds_h, id_d=id_h)


def kernel(L0_attn_x, L0_attn_Wqkv, L0_attn_bqkv, L0_attn_Wo, L0_attn_bo,
           L0_ffn_gate_wq, L0_ffn_up_wq, L0_ffn_down_wq,
           L0_ffn_gate_scale, L0_ffn_up_scale, L0_ffn_down_scale):
    x = np.asarray(L0_attn_x, dtype=np.float32)
    w = _prep_weights(L0_attn_Wqkv, L0_attn_bqkv, L0_attn_Wo, L0_attn_bo,
                      L0_ffn_gate_wq, L0_ffn_up_wq, L0_ffn_down_wq,
                      L0_ffn_gate_scale, L0_ffn_up_scale, L0_ffn_down_scale)
    nc = build_nc(n_sub=3)
    in_maps = []
    for c in range(NCORES):
        b, g = c // 4, c % 4
        m = dict(w)
        m["xin"] = np.ascontiguousarray(x[b, g * T:(g + 1) * T, :])
        in_maps.append(m)
    res = run_bass_kernel_spmd(nc, in_maps, core_ids=list(range(NCORES)))
    out = np.empty((B, S, H), dtype=np.float32)
    for c in range(NCORES):
        b, g = c // 4, c % 4
        out[b, g * T:(g + 1) * T, :] = np.asarray(res.results[c]["yout"])
    return out


# version 15 (final)
# speedup vs baseline: 1.0048x; 1.0048x over previous
import sys
if "/opt/trn_rl_repo" not in sys.path:
    sys.path.insert(0, "/opt/trn_rl_repo")

import numpy as np
import ml_dtypes
from contextlib import ExitStack

import concourse.bass as bass
import concourse.mybir as mybir
from concourse.bass_utils import run_bass_kernel_spmd

F32 = mybir.dt.float32
BF16 = mybir.dt.bfloat16
AF = mybir.ActivationFunctionType
ALU = mybir.AluOpType
AX = mybir.AxisListType

NCORES = 8
B, S, H = 2, 4096, 4096
T = 1024
CH = 512
NTT = CH // 128
HB = H // 128
ITN = 86
I = 11008
HCN = H // 512
D = 1024
DT = D // 128
EPS = 1e-5

SUM, SSQ0, SSQ, MEAN, MSQ, VAR, LNV, RSTD, NB0, NBIAS = 0, 1, 9, 10, 11, 12, 13, 14, 15, 16
AMX, AMXS, ADEN, AR0, ANT, ADINV = 0, 1, 2, 3, 4, 5


class Sem:
    def __init__(self, h):
        self.h = h
        self.n = 0

    def inc(self, k=1):
        self.n += k
        return self.n


class Prog:
    ENGINES = ("sync", "gpsimd", "vector", "scalar", "tensor")

    def __init__(self):
        self.q = {e: [] for e in self.ENGINES}

    def emit(self, eng, fn):
        self.q[eng].append(fn)

    def wait(self, eng, sem, val):
        if val > 0:
            self.emit(eng, lambda e, s=sem.h, v=val: e.wait_ge(s, v))


def build_nc(n_sub=3, dbg=False):
    nc = bass.Bass("TRN2", target_bir_lowering=False)

    xin = nc.dram_tensor("xin", [T, H], F32, kind="ExternalInput").ap()
    wg_d = nc.dram_tensor("wg_d", [ITN, 128, HB * 128], BF16, kind="ExternalInput").ap()
    wu_d = nc.dram_tensor("wu_d", [ITN, 128, HB * 128], BF16, kind="ExternalInput").ap()
    wd_d = nc.dram_tensor("wd_d", [HCN, 43, 128, 1024], BF16, kind="ExternalInput").ap()
    wqk_d = nc.dram_tensor("wqk_d", [16, 128, HB * 128], F32, kind="ExternalInput").ap()
    wv_d = nc.dram_tensor("wv_d", [HB, 128, D], F32, kind="ExternalInput").ap()
    wo_d = nc.dram_tensor("wo_d", [HCN, 128, DT * 512], F32, kind="ExternalInput").ap()
    gs_d = nc.dram_tensor("gs_d", [128, ITN], F32, kind="ExternalInput").ap()
    us_d = nc.dram_tensor("us_d", [128, ITN], F32, kind="ExternalInput").ap()
    bqkv_d = nc.dram_tensor("bqkv_d", [128, 24], F32, kind="ExternalInput").ap()
    bqv_d = nc.dram_tensor("bqv_d", [1, D], F32, kind="ExternalInput").ap()
    bo_d = nc.dram_tensor("bo_d", [1, H], F32, kind="ExternalInput").ap()
    ds_d = nc.dram_tensor("ds_d", [1, H], F32, kind="ExternalInput").ap()
    id_d = nc.dram_tensor("id_d", [128, 128], BF16, kind="ExternalInput").ap()
    yout = nc.dram_tensor("yout", [T, H], F32, kind="ExternalOutput").ap()
    if dbg:
        dbg_stats = nc.dram_tensor("dbg_stats", [128, 224], F32, kind="ExternalOutput").ap()
        dbg_xlnT = nc.dram_tensor("dbg_xlnT", [128, HB * CH], BF16, kind="ExternalOutput").ap()
        dbg_inter = nc.dram_tensor("dbg_inter", [128, ITN * CH], BF16, kind="ExternalOutput").ap()
        dbg_stats2 = nc.dram_tensor("dbg_stats2", [128, 224], F32, kind="ExternalOutput").ap()
        dbg_xlnT2 = nc.dram_tensor("dbg_xlnT2", [128, HB * CH], BF16, kind="ExternalOutput").ap()
    if dbg == 2:
        dbg_r1 = nc.dram_tensor("dbg_r1", [T, H], F32, kind="ExternalOutput").ap()
        dbg_ccin = nc.dram_tensor("dbg_ccin", [2048, D], BF16, kind="ExternalOutput").ap()
        dbg_ccout = nc.dram_tensor("dbg_ccout", [8192, D], BF16, kind="ExternalOutput").ap()
        dbg_O = nc.dram_tensor("dbg_O", [128, 16384], BF16, kind="ExternalOutput").ap()
        dbg_astats = nc.dram_tensor("dbg_astats", [128, 224], F32, kind="ExternalOutput").ap()

    r1 = nc.dram_tensor("r1", [T, H], F32, kind="Internal").ap()
    r2 = nc.dram_tensor("r2", [T, H], F32, kind="Internal").ap()
    cc_in_t = nc.dram_tensor("cc_in", [2048, D], BF16, kind="Internal")
    cc_out_t = nc.dram_tensor("cc_out", [4 * 2048, D], BF16, kind="Internal")
    cc_in = cc_in_t.ap()
    cc_out = cc_out_t.ap()

    es = ExitStack()
    xlnT = es.enter_context(nc.sbuf_tensor("xlnT", [128, HB * CH], BF16))
    inter = es.enter_context(nc.sbuf_tensor("inter", [128, ITN * CH], BF16))
    wgu = es.enter_context(nc.sbuf_tensor("wgu", [128, 16384], BF16))
    wdb = es.enter_context(nc.sbuf_tensor("wdb", [128, 3072], BF16))
    lnsc = es.enter_context(nc.sbuf_tensor("lnsc", [128, 1024], BF16))
    srcln = es.enter_context(nc.sbuf_tensor("srcln", [128, 8192], F32))
    srcadd = es.enter_context(nc.sbuf_tensor("srcadd", [128, 1024], F32))
    dstst = es.enter_context(nc.sbuf_tensor("dstst", [128, 1024], F32))
    evtmp = es.enter_context(nc.sbuf_tensor("evtmp", [128, 512], F32))
    bcast = es.enter_context(nc.sbuf_tensor("bcast", [128, 512], F32))
    silu_s = es.enter_context(nc.sbuf_tensor("silu_s", [128, 1024], BF16))
    stats = es.enter_context(nc.sbuf_tensor("stats", [128, 192], F32))
    cst = es.enter_context(nc.sbuf_tensor("cst", [128, 4], F32))
    gs_s = es.enter_context(nc.sbuf_tensor("gs_s", [128, ITN], F32))
    us_s = es.enter_context(nc.sbuf_tensor("us_s", [128, ITN], F32))
    bqkv_s = es.enter_context(nc.sbuf_tensor("bqkv_s", [128, 24], F32))
    id_s = es.enter_context(nc.sbuf_tensor("id_s", [128, 128], BF16))

    pstr = [es.enter_context(nc.psum_tensor("pstr%d" % i, [128, 1024], BF16)) for i in range(2)]
    psf = [es.enter_context(nc.psum_tensor("psf%d" % i, [128, 512], F32)) for i in range(6)]

    sWL = Sem(es.enter_context(nc.semaphore("sWL")))
    sGL = Sem(es.enter_context(nc.semaphore("sGL")))
    sST = Sem(es.enter_context(nc.semaphore("sST")))
    semI = Sem(es.enter_context(nc.semaphore("semI")))
    semWgu = [Sem(es.enter_context(nc.semaphore("semWgu%d" % i))) for i in range(2)]
    semWd = [Sem(es.enter_context(nc.semaphore("semWd%d" % i))) for i in range(3)]
    semSrcln = [Sem(es.enter_context(nc.semaphore("semSrcln%d" % i))) for i in range(2)]
    semSrcadd = [Sem(es.enter_context(nc.semaphore("semSrcadd%d" % i))) for i in range(2)]
    semBcG = Sem(es.enter_context(nc.semaphore("semBcG")))
    semBcS = Sem(es.enter_context(nc.semaphore("semBcS")))
    semWguG = [Sem(es.enter_context(nc.semaphore("semWguG%d" % i))) for i in range(2)]
    semWdG = [Sem(es.enter_context(nc.semaphore("semWdG%d" % i))) for i in range(3)]
    semK = Sem(es.enter_context(nc.semaphore("semK")))
    semSt = [Sem(es.enter_context(nc.semaphore("semSt%d" % i))) for i in range(2)]
    semSilu = [Sem(es.enter_context(nc.semaphore("semSilu%d" % i))) for i in range(2)]
    sCC = Sem(es.enter_context(nc.semaphore("sCC")))
    sPE = Sem(es.enter_context(nc.semaphore("sPE")))
    sDV = Sem(es.enter_context(nc.semaphore("sDV")))
    sAC = Sem(es.enter_context(nc.semaphore("sAC")))
    sAC2 = Sem(es.enter_context(nc.semaphore("sAC2")))

    P = Prog()

    def dma(eng, sem, out_ap, in_ap):
        v = sem.inc(16)
        P.emit(eng, lambda e, o=out_ap, i=in_ap, s=sem.h: e.dma_start(out=o, in_=i).then_inc(s, 16))
        return v

    def mm(out_ap, lhsT_ap, rhs_ap, start, stop, inc=False):
        if inc:
            v = sPE.inc(1)
            P.emit("tensor", lambda e, o=out_ap, l=lhsT_ap, r=rhs_ap, a=start, b=stop, s=sPE.h:
                   e.matmul(o, l, r, start=a, stop=b).then_inc(s, 1))
            return v
        P.emit("tensor", lambda e, o=out_ap, l=lhsT_ap, r=rhs_ap, a=start, b=stop:
               e.matmul(o, l, r, start=a, stop=b))
        return sPE.n

    def trans(out_ap, in_ap, inc=False):
        if inc:
            v = sPE.inc(1)
            P.emit("tensor", lambda e, o=out_ap, i=in_ap, s=sPE.h:
                   e.transpose(o, i, id_s[:, :]).then_inc(s, 1))
            return v
        P.emit("tensor", lambda e, o=out_ap, i=in_ap: e.transpose(o, i, id_s[:, :]))
        return sPE.n

    def pe_flush():
        v = sPE.inc(1)
        P.emit("tensor", lambda e, s=sPE.h:
               e.matmul(psf[5][:, 0:128], id_s[:, :], id_s[:, 0:128],
                        start=True, stop=True).then_inc(s, 1))
        return v

    def dve(fn_name, *aps, **kw):
        v = sDV.inc(1)
        P.emit("vector", lambda e, n=fn_name, a=aps, k=kw, s=sDV.h:
               getattr(e, n)(*a, **k).then_inc(s, 1))
        return v

    def act(fn_name, *aps, **kw):
        v = sAC.inc(1)
        P.emit("scalar", lambda e, n=fn_name, a=aps, k=kw, s=sAC.h:
               getattr(e, n)(*a, **k).then_inc(s, 1))
        return v

    dma("sync", semI, gs_s[:, :], gs_d)
    dma("sync", semI, us_s[:, :], us_d)
    dma("sync", semI, bqkv_s[:, :], bqkv_d)
    init_loads = dma("sync", semI, id_s[:, :], id_d)
    P.wait("tensor", semI, init_loads)
    P.wait("scalar", semI, init_loads)
    P.wait("vector", semI, init_loads)
    P.emit("vector", lambda e: e.memset(cst[:, 0:1], EPS))
    P.emit("vector", lambda e: e.memset(cst[:, 1:2], -0.5))
    cst_dv = sDV.inc(1)
    P.emit("vector", lambda e, s=sDV.h: e.memset(cst[:, 2:3], 1.0 / 32.0).then_inc(s, 1))
    P.wait("scalar", sDV, cst_dv)

    st = {
        "xlnT_pe_done": 0,
        "inter_pe_done": 0,
        "wgu_pe_done": 0,
        "wd_pe_done": 0,
        "lnsc_pe": [0, 0],
        "pstr_ac": [0, 0],
        "psf_free_dv": [0] * 6,
        "psf_free_ac": [0] * 6,
        "srcln_free": [(None, 0), (None, 0)],
        "silu_free": [(None, 0), (None, 0)],
        "wgu_par_pe": [0, 0],
        "wd_par_pe": [0, 0, 0],
        "srcadd_free": [(None, 0), (None, 0)],
        "dstst_free": [(None, 0), (None, 0)],
        "bcast_dv": 0,
        "evtmp_dv": 0,
        "stores_all": 0,
    }

    def wtup(eng, tup):
        sem, val = tup
        if sem is not None and val > 0:
            P.wait(eng, sem, val)

    def ln_chunk(src, cc):
        for tt in range(NTT):
            R = cc * CH + tt * 128
            par = tt % 2
            sl = srcln[:, par * 4096:(par + 1) * 4096]
            so = tt * 32
            wtup("gpsimd", st["srcln_free"][par])
            ld = dma("gpsimd", semSrcln[par], sl, src[R:R + 128, :])
            P.wait("vector", semSrcln[par], ld)
            dve("reduce_sum", stats[:, so + SUM:so + SUM + 1], sl, axis=AX.X)
            P.wait("scalar", semSrcln[par], ld)
            sq_last = 0
            sq_prev = [0, 0]
            for q in range(8):
                qh = lnsc[:, (q % 2) * 512:(q % 2 + 1) * 512]
                P.wait("scalar", sPE, st["lnsc_pe"][q % 2])
                P.wait("scalar", sAC, sq_prev[q % 2])
                sq_last = act("activation", qh, sl[:, q * 512:(q + 1) * 512], AF.Square,
                              accum_out=stats[:, so + SSQ0 + q:so + SSQ0 + q + 1])
                sq_prev[q % 2] = sq_last
            P.wait("vector", sAC, sq_last)
            v = dve("reduce_sum", stats[:, so + SSQ:so + SSQ + 1],
                    stats[:, so + SSQ0:so + SSQ0 + 8], axis=AX.X)
            P.wait("vector", sDV, v)
            v = dve("tensor_scalar_mul", stats[:, so + MEAN:so + MEAN + 1],
                    stats[:, so + SUM:so + SUM + 1], 1.0 / H)
            P.wait("vector", sDV, v)
            v = dve("tensor_mul", stats[:, so + MSQ:so + MSQ + 1],
                    stats[:, so + MEAN:so + MEAN + 1], stats[:, so + MEAN:so + MEAN + 1])
            P.wait("vector", sDV, v)
            v = dve("scalar_tensor_tensor", stats[:, so + VAR:so + VAR + 1],
                    stats[:, so + SSQ:so + SSQ + 1], 1.0 / H,
                    stats[:, so + MSQ:so + MSQ + 1], ALU.mult, ALU.subtract)
            P.wait("scalar", sDV, v)
            v = act("activation", stats[:, so + LNV:so + LNV + 1],
                    stats[:, so + VAR:so + VAR + 1], AF.Ln, bias=cst[:, 0:1])
            P.wait("scalar", sAC, v)
            v = act("activation", stats[:, so + RSTD:so + RSTD + 1],
                    stats[:, so + LNV:so + LNV + 1], AF.Exp, scale=cst[:, 1:2])
            P.wait("vector", sAC, v)
            v = dve("tensor_mul", stats[:, so + NB0:so + NB0 + 1],
                    stats[:, so + MEAN:so + MEAN + 1], stats[:, so + RSTD:so + RSTD + 1])
            P.wait("vector", sDV, v)
            nb_done = dve("tensor_scalar_mul", stats[:, so + NBIAS:so + NBIAS + 1],
                          stats[:, so + NB0:so + NB0 + 1], -1.0)
            last_dv = nb_done
            for q in range(8):
                qh = lnsc[:, (q % 2) * 512:(q % 2 + 1) * 512]
                P.wait("vector", sDV, last_dv)
                P.wait("vector", sPE, st["lnsc_pe"][q % 2])
                nv = dve("tensor_scalar", qh, sl[:, q * 512:(q + 1) * 512],
                         stats[:, so + RSTD:so + RSTD + 1],
                         stats[:, so + NBIAS:so + NBIAS + 1], ALU.mult, ALU.add)
                bank = q % 2
                last_oct = (q == 7 and tt == NTT - 1)
                P.wait("tensor", sDV, nv if last_oct else nv + 1)
                P.wait("tensor", sAC, st["pstr_ac"][bank])
                for k in range(4):
                    trans(pstr[bank][:, k * 128:(k + 1) * 128],
                          qh[:, k * 128:(k + 1) * 128])
                pe_v = pe_flush()
                st["lnsc_pe"][q % 2] = pe_v
                dest = xlnT[:, :].rearrange("p (hb t) -> p hb t", t=CH)[:, 4 * q:4 * q + 4,
                                                                       tt * 128:(tt + 1) * 128]
                srcv = pstr[bank][:, 0:512].rearrange("p (k t) -> p k t", t=128)
                P.wait("scalar", sPE, pe_v)
                ev = act("copy", dest, srcv)
                st["pstr_ac"][bank] = ev
            st["srcln_free"][par] = (sDV, sDV.n)
        return sAC.n

    def gateup_chunk(xln_ready_ac):
        for it in range(ITN):
            par = it % 2
            wslab = wgu[:, par * 8192:par * 8192 + 8192]
            P.wait("sync", sPE, st["wgu_par_pe"][par])
            lwg = dma("sync", semWgu[par], wslab[:, 0:4096], wg_d[it])
            lwu = dma("sync", semWgu[par], wslab[:, 4096:8192], wu_d[it])
            pg, pu = psf[2 * par], psf[2 * par + 1]
            pgi, pui = 2 * par, 2 * par + 1
            P.wait("tensor", semWgu[par], lwu)
            if it < 2:
                P.wait("tensor", sAC, xln_ready_ac)
            P.wait("tensor", sAC, max(st["psf_free_ac"][pgi], st["psf_free_ac"][pui]))
            P.wait("tensor", sDV, max(st["psf_free_dv"][pgi], st["psf_free_dv"][pui]))
            for hb in range(HB):
                mm(pg[:, :], wslab[:, hb * 128:(hb + 1) * 128],
                   xlnT[:, hb * CH:(hb + 1) * CH], start=(hb == 0), stop=(hb == HB - 1))
            for hb in range(HB):
                mm(pu[:, :], wslab[:, 4096 + hb * 128:4096 + (hb + 1) * 128],
                   xlnT[:, hb * CH:(hb + 1) * CH], start=(hb == 0), stop=(hb == HB - 1))
            pe_v = pe_flush()
            st["wgu_par_pe"][par] = pe_v
            sp = it % 2
            sh = silu_s[:, sp * 512:(sp + 1) * 512]
            P.wait("scalar", sPE, pe_v)
            wtup("scalar", st["silu_free"][sp])
            sv = act("activation", sh, pg[:, :], AF.Silu, scale=gs_s[:, it:it + 1])
            st["psf_free_ac"][pgi] = sv
            P.wait("vector", sAC, sv)
            iv = dve("scalar_tensor_tensor", inter[:, it * CH:(it + 1) * CH],
                     pu[:, :], us_s[:, it:it + 1], sh, ALU.mult, ALU.mult)
            st["psf_free_dv"][pui] = iv
            st["silu_free"][sp] = (sDV, iv)
        return sDV.n

    def down_chunk(src, dst, cc, inter_done_dv):
        first = True
        for hc in range(HCN):
            P.wait("gpsimd", sDV, st["bcast_dv"])
            bc = dma("gpsimd", semBcG, bcast[:, :],
                     ds_d[0:1, hc * 512:(hc + 1) * 512].broadcast_to([128, 512]))
            if first:
                P.wait("tensor", sDV, inter_done_dv)
                first = False
            for tt in range(NTT):
                P.wait("tensor", sDV, st["psf_free_dv"][tt])
                P.wait("tensor", sAC, st["psf_free_ac"][tt])
            hc_pe = 0
            for itp in range(43):
                par = itp % 3
                P.wait("sync", sPE, st["wd_par_pe"][par])
                lw = dma("sync", semWd[par], wdb[:, par * 1024:(par + 1) * 1024], wd_d[hc, itp])
                P.wait("tensor", semWd[par], lw)
                for k in range(2):
                    it = 2 * itp + k
                    for tt in range(NTT):
                        last = (k == 1) and (tt == NTT - 1)
                        pe_v = mm(psf[tt][:, :],
                                  inter[:, it * CH + tt * 128:it * CH + (tt + 1) * 128],
                                  wdb[:, par * 1024 + k * 512:par * 1024 + (k + 1) * 512],
                                  start=(it == 0), stop=(it == I // 128 - 1),
                                  inc=last)
                st["wd_par_pe"][par] = pe_v
            hc_pe = pe_flush()
            for tt in range(NTT):
                R = cc * CH + tt * 128
                u = hc * NTT + tt
                spar = u % 2
                wtup("gpsimd", st["srcadd_free"][spar])
                sa = dma("gpsimd", semSrcadd[spar], srcadd[:, spar * 512:(spar + 1) * 512],
                         src[R:R + 128, hc * 512:(hc + 1) * 512])
                P.wait("vector", sPE, hc_pe)
                P.wait("vector", semBcG, bc)
                P.wait("vector", sDV, st["evtmp_dv"])
                t1 = dve("tensor_tensor", evtmp[:, :], psf[tt][:, :], bcast[:, :], ALU.mult)
                st["bcast_dv"] = t1
                P.wait("vector", semSrcadd[spar], sa)
                wtup("vector", st["dstst_free"][spar])
                P.wait("vector", sDV, t1)
                rn = dve("scalar_tensor_tensor", dstst[:, spar * 512:(spar + 1) * 512],
                         evtmp[:, :], 0.5, srcadd[:, spar * 512:(spar + 1) * 512],
                         ALU.mult, ALU.add)
                st["evtmp_dv"] = rn
                st["psf_free_dv"][tt] = rn
                st["srcadd_free"][spar] = (sDV, rn)
                P.wait("gpsimd", sDV, rn)
                sv = dma("gpsimd", semSt[spar], dst[R:R + 128, hc * 512:(hc + 1) * 512],
                         dstst[:, spar * 512:(spar + 1) * 512])
                st["dstst_free"][spar] = (semSt[spar], sv)

    def ffn_phase(src, dst, dump=False):
        xr0 = ln_chunk(src, 0)
        gu0 = gateup_chunk(xr0)
        st["xlnT_pe_done"] = sPE.n
        xr1 = ln_chunk(src, 1)
        down_chunk(src, dst, 0, gu0)
        gu1 = gateup_chunk(xr1)
        st["xlnT_pe_done"] = sPE.n
        down_chunk(src, dst, 1, gu1)
        st["inter_pe_done"] = sPE.n

    def attn_phase(src, dst):
        QOFF = 8 * 4096
        q_evac_ac = 0
        ccv = None
        for cc in range(2):
            xr = ln_chunk(src, cc)
            for ft in range(8, 16):
                _qk_tile(ft, cc, xr, is_k=True)
            _v_chunk(cc, xr)
            if cc == 1:
                P.wait("gpsimd", semSilu[0], semSilu[0].n)
                P.wait("gpsimd", semSilu[1], semSilu[1].n)
                ccv = sCC.inc(1)
                P.emit("gpsimd", lambda e, s=sCC.h: e.collective_compute(
                    "AllGather", ALU.bypass,
                    replica_groups=[[0, 1, 2, 3], [4, 5, 6, 7]],
                    ins=[cc_in_t.ap().opt()], outs=[cc_out_t.ap().opt()]).then_inc(s))
            for ft in range(0, 8):
                _qk_tile(ft, cc, xr, is_k=False)
            st["xlnT_pe_done"] = sPE.n
        q_evac_ac = sDV.n

        if dbg == 2:
            P.wait("gpsimd", sCC, ccv)
            dma("gpsimd", sST, dbg_r1, src)
            dma("gpsimd", sST, dbg_ccin, cc_in)
            dma("gpsimd", sST, dbg_ccout, cc_out)
        P.wait("sync", sCC, ccv)
        P.wait("sync", sPE, st["inter_pe_done"])
        kload = 0
        for dt in range(DT):
            for rk in range(4):
                kload = dma("sync", semK,
                            inter[:, dt * 4096 + rk * 1024: dt * 4096 + (rk + 1) * 1024],
                            cc_out[rk * 2048 + dt * 128: rk * 2048 + (dt + 1) * 128, :])

        av_pe_done = 0
        for qg in range(4):
            for q2 in range(2):
                _scores(2 * qg + q2, kload, q_evac_ac)
            for q2 in range(2):
                _softmax_attnT(2 * qg + q2)
            _attn_v(qg)
        av_pe_done = sPE.n
        o_evac_done = sAC.n

        if dbg == 2:
            P.wait("gpsimd", sAC, o_evac_done)
            P.wait("gpsimd", sDV, sDV.n)
            dma("gpsimd", sST, dbg_O, xlnT[:, :])
            dma("gpsimd", sST, dbg_astats, stats[:, :])
        P.wait("tensor", sAC, o_evac_done)
        for qti in range(8):
            for dt in range(DT):
                bank = (qti * DT + dt) // 8 % 2
                slot = (qti * DT + dt) % 8
                if slot == 0:
                    P.wait("tensor", sAC, st["pstr_ac"][bank])
                trans(pstr[bank][:, slot * 128:(slot + 1) * 128],
                      xlnT[:, qti * 1024 + dt * 128: qti * 1024 + (dt + 1) * 128])
                if slot == 7:
                    pe_v = pe_flush()
                    base = (qti * DT + dt) - 7
                    P.wait("scalar", sPE, pe_v)
                    for j in range(8):
                        g = base + j
                        qq, dd = g // DT, g % DT
                        ev = act("copy",
                                 xlnT[:, 8192 + dd * 1024 + qq * 128: 8192 + dd * 1024 + (qq + 1) * 128],
                                 pstr[bank][:, j * 128:(j + 1) * 128])
                    st["pstr_ac"][bank] = ev
        ot_ac = sAC.n

        for hc in range(HCN):
            par = hc % 2
            wslab = wgu[:, par * 8192: par * 8192 + 4096]
            P.wait("gpsimd", sPE, max(av_pe_done, st["wgu_par_pe"][par]))
            lw = dma("gpsimd", semWguG[par], wslab, wo_d[hc])
            P.wait("sync", sDV, st["bcast_dv"])
            bc = dma("sync", semBcS, bcast[:, :],
                     bo_d[0:1, hc * 512:(hc + 1) * 512].broadcast_to([128, 512]))
            for qti in range(8):
                so = 128 + qti * 8
                bank = qti % 2
                P.wait("tensor", semWguG[par], lw)
                P.wait("tensor", sAC, ot_ac)
                P.wait("tensor", sDV, st["psf_free_dv"][bank])
                P.wait("tensor", sAC, st["psf_free_ac"][bank])
                for dt in range(DT):
                    mm(psf[bank][:, :],
                       xlnT[:, 8192 + dt * 1024 + qti * 128: 8192 + dt * 1024 + (qti + 1) * 128],
                       wslab[:, dt * 512:(dt + 1) * 512],
                       start=(dt == 0), stop=(dt == DT - 1))
                pe_v = pe_flush()
                st["wgu_par_pe"][par] = pe_v
                R = qti * 128
                u = hc * 8 + qti
                spar = u % 2
                wtup("gpsimd", st["srcadd_free"][spar])
                sa = dma("gpsimd", semSrcadd[spar], srcadd[:, spar * 512:(spar + 1) * 512],
                         src[R:R + 128, hc * 512:(hc + 1) * 512])
                P.wait("vector", sPE, pe_v)
                P.wait("vector", semBcS, bc)
                P.wait("vector", sDV, max(st["evtmp_dv"], st["dinv_dv"][qti]))
                t2 = dve("scalar_tensor_tensor", evtmp[:, :], psf[bank][:, :],
                         stats[:, so + ADINV:so + ADINV + 1], bcast[:, :],
                         ALU.mult, ALU.add)
                st["psf_free_dv"][bank] = t2
                st["bcast_dv"] = t2
                P.wait("vector", semSrcadd[spar], sa)
                wtup("vector", st["dstst_free"][spar])
                P.wait("vector", sDV, t2)
                rn = dve("tensor_add", dstst[:, spar * 512:(spar + 1) * 512],
                         evtmp[:, :], srcadd[:, spar * 512:(spar + 1) * 512])
                st["evtmp_dv"] = rn
                st["srcadd_free"][spar] = (sDV, rn)
                P.wait("gpsimd", sDV, rn)
                sv = dma("gpsimd", semSt[spar], dst[R:R + 128, hc * 512:(hc + 1) * 512],
                         dstst[:, spar * 512:(spar + 1) * 512])
                st["dstst_free"][spar] = (semSt[spar], sv)
        st["inter_pe_done"] = sPE.n

    def _qk_tile(ft, cc, xln_ready_ac, is_k):
        QOFF = 8 * 4096
        par = ft % 2
        wslab = wgu[:, par * 8192: par * 8192 + 4096]
        P.wait("gpsimd", sPE, st["wgu_par_pe"][par])
        lw = dma("gpsimd", semWguG[par], wslab, wqk_d[ft])
        bank = ft % 2
        pb = psf[bank]
        P.wait("tensor", semWguG[par], lw)
        P.wait("tensor", sAC, xln_ready_ac)
        P.wait("tensor", sAC, st["psf_free_ac"][bank])
        P.wait("tensor", sDV, st["psf_free_dv"][bank])
        for cb in range(HB):
            mm(pb[:, :], wslab[:, cb * 128:(cb + 1) * 128],
               xlnT[:, cb * CH:(cb + 1) * CH],
               start=(cb == 0), stop=(cb == HB - 1))
        pe_v = pe_flush()
        st["wgu_par_pe"][par] = pe_v
        P.wait("vector", sPE, pe_v)
        if is_k:
            dt = ft - 8
            sp = ft % 2
            sh = silu_s[:, sp * 512:(sp + 1) * 512]
            wtup("vector", st["silu_free"][sp])
            ev = dve("tensor_scalar_add", sh, pb[:, :], bqkv_s[:, ft:ft + 1])
            st["psf_free_dv"][bank] = ev
            P.wait("gpsimd", sDV, ev)
            sv = dma("gpsimd", semSilu[sp], cc_in[dt * 128:(dt + 1) * 128, cc * 512:(cc + 1) * 512], sh)
            st["silu_free"][sp] = (semSilu[sp], sv)
        else:
            dest = inter[:, QOFF + ft * 1024 + cc * 512: QOFF + ft * 1024 + (cc + 1) * 512]
            ev = dve("tensor_scalar_add", dest, pb[:, :], bqkv_s[:, ft:ft + 1])
            st["psf_free_dv"][bank] = ev

    def _v_chunk(cc, xln_ready_ac):
        for dc in range(2):
            P.wait("sync", sDV, st["bcast_dv"])
            bc = dma("sync", semBcS, bcast[:, :],
                     bqv_d[0:1, dc * 512:(dc + 1) * 512].broadcast_to([128, 512]))
            for tt in range(NTT):
                P.wait("tensor", sDV, st["psf_free_dv"][tt])
                P.wait("tensor", sAC, st["psf_free_ac"][tt])
            vv_pe = 0
            for cb in range(HB):
                par = cb % 3
                P.wait("gpsimd", sPE, st["wd_par_pe"][par])
                lw = dma("gpsimd", semWdG[par], wdb[:, par * 1024:(par + 1) * 1024], wv_d[cb])
                P.wait("tensor", semWdG[par], lw)
                if cb < 2:
                    P.wait("tensor", sAC, xln_ready_ac)
                for tt in range(NTT):
                    pe_v = mm(psf[tt][:, :],
                              xlnT[:, cb * CH + tt * 128: cb * CH + (tt + 1) * 128],
                              wdb[:, par * 1024 + dc * 512: par * 1024 + (dc + 1) * 512],
                              start=(cb == 0), stop=(cb == HB - 1), inc=(tt == NTT - 1))
                st["wd_par_pe"][par] = pe_v
            vv_pe = pe_flush()
            st["wd_par_pe"] = [vv_pe] * 3
            for tt in range(NTT):
                sp = tt % 2
                sh = silu_s[:, sp * 512:(sp + 1) * 512]
                P.wait("vector", sPE, vv_pe)
                P.wait("vector", semBcS, bc)
                wtup("vector", st["silu_free"][sp])
                ev = dve("tensor_tensor", sh, psf[tt][:, :], bcast[:, :], ALU.add)
                st["psf_free_dv"][tt] = ev
                st["bcast_dv"] = ev
                P.wait("gpsimd", sDV, ev)
                sv = dma("gpsimd", semSilu[sp],
                         cc_in[1024 + cc * 512 + tt * 128: 1024 + cc * 512 + (tt + 1) * 128,
                               dc * 512:(dc + 1) * 512], sh)
                st["silu_free"][sp] = (semSilu[sp], sv)

    def _scores(qti, kload, q_evac_ac):
        QOFF = 8 * 4096
        par = qti % 2
        sco = srcln[:, par * 4096:(par + 1) * 4096]
        ev_dv = 0
        for kc in range(8):
            bank = kc % 4
            P.wait("tensor", semK, kload)
            if qti == 0 and kc == 0:
                P.wait("tensor", sDV, q_evac_ac)
            P.wait("tensor", sDV, st["psf_free_dv"][bank])
            P.wait("tensor", sAC, st["psf_free_ac"][bank])
            for dt in range(DT):
                mm(psf[bank][:, :],
                   inter[:, QOFF + dt * 1024 + qti * 128: QOFF + dt * 1024 + (qti + 1) * 128],
                   inter[:, dt * 4096 + kc * 512: dt * 4096 + (kc + 1) * 512],
                   start=(dt == 0), stop=(dt == DT - 1))
            pe_v = pe_flush()
            P.wait("vector", sPE, pe_v)
            wtup("vector", st["srcln_free"][par] if kc == 0 else (None, 0))
            ev_dv = dve("tensor_copy", sco[:, kc * 512:(kc + 1) * 512], psf[bank][:, :])
            st["psf_free_dv"][bank] = ev_dv
        st["srcln_free"][par] = (sDV, ev_dv)
        st.setdefault("sco_dv", [0, 0])[par] = ev_dv

    def _softmax_attnT(qti):
        par = qti % 2
        sco = srcln[:, par * 4096:(par + 1) * 4096]
        so = 128 + qti * 8
        ev_dv = st["sco_dv"][par]
        P.wait("vector", sDV, ev_dv)
        v = dve("reduce_max", stats[:, so + AMX:so + AMX + 1], sco, axis=AX.X)
        P.wait("vector", sDV, v)
        v = dve("tensor_scalar_mul", stats[:, so + AMXS:so + AMXS + 1],
                stats[:, so + AMX:so + AMX + 1], -1.0 / 32.0)
        expd = wgu[:, 8192 + par * 4096: 8192 + (par + 1) * 4096]
        P.wait("scalar", sDV, v)
        P.wait("scalar", sPE, st.setdefault("expS_pe", [0, 0])[par])
        exv = act("activation", expd, sco, AF.Exp, scale=cst[:, 2:3],
                  bias=stats[:, so + AMXS:so + AMXS + 1],
                  accum_out=stats[:, so + ADEN:so + ADEN + 1])
        st["srcln_free"][par] = (sAC, exv)
        P.wait("vector", sAC, exv)
        v = dve("reciprocal", stats[:, so + AR0:so + AR0 + 1], stats[:, so + ADEN:so + ADEN + 1])
        P.wait("vector", sDV, v)
        v = dve("tensor_mul", stats[:, so + ANT:so + ANT + 1],
                stats[:, so + ADEN:so + ADEN + 1], stats[:, so + AR0:so + AR0 + 1])
        P.wait("vector", sDV, v)
        v = dve("tensor_scalar", stats[:, so + ANT:so + ANT + 1],
                stats[:, so + ANT:so + ANT + 1], -1.0, 2.0, ALU.mult, ALU.add)
        P.wait("vector", sDV, v)
        v = dve("tensor_mul", stats[:, so + ADINV:so + ADINV + 1],
                stats[:, so + AR0:so + AR0 + 1], stats[:, so + ANT:so + ANT + 1])
        st.setdefault("dinv_dv", [0] * 8)[qti] = v
        for g in range(4):
            bank = g % 2
            P.wait("tensor", sAC, max(exv, st["pstr_ac"][bank]))
            P.wait("tensor", sPE, st.setdefault("attnT_pe", 0))
            for k in range(8):
                kb = g * 8 + k
                trans(pstr[bank][:, k * 128:(k + 1) * 128],
                      expd[:, kb * 128:(kb + 1) * 128])
            pe_v = pe_flush()
            st["expS_pe"][par] = pe_v
            dest = wgu[:, :].rearrange("p (kb c) -> p kb c", c=256)[
                :, g * 8:(g + 1) * 8, (qti % 2) * 128:(qti % 2 + 1) * 128]
            srcv = pstr[bank][:, :].rearrange("p (k t) -> p k t", t=128)
            P.wait("scalar", sPE, pe_v)
            ev = act("copy", dest, srcv)
            st["pstr_ac"][bank] = ev
        st["attnT_ac"] = sAC.n

    def _attn_v(qg):
        for bank in range(4):
            P.wait("tensor", sAC, st["psf_free_ac"][bank])
            P.wait("tensor", sDV, st["psf_free_dv"][bank])
        P.wait("tensor", sAC, st["attnT_ac"])
        av_pe = 0
        for kb in range(HB):
            rk, lt = kb // 8, kb % 8
            par = kb % 3
            P.wait("sync", sCC, 1)
            P.wait("sync", sPE, st["wd_par_pe"][par])
            lw = dma("sync", semWd[par], wdb[:, par * 1024:(par + 1) * 1024],
                     cc_out[rk * 2048 + 1024 + lt * 128: rk * 2048 + 1024 + (lt + 1) * 128, :])
            P.wait("tensor", semWd[par], lw)
            for q2 in range(2):
                for dh in range(2):
                    pe_v = mm(psf[q2 * 2 + dh][:, :],
                              wgu[:, kb * 256 + q2 * 128: kb * 256 + (q2 + 1) * 128],
                              wdb[:, par * 1024 + dh * 512: par * 1024 + (dh + 1) * 512],
                              start=(kb == 0), stop=(kb == HB - 1),
                              inc=(q2 == 1 and dh == 1))
            st["wd_par_pe"][par] = pe_v
        av_pe = pe_flush()
        st["wd_par_pe"] = [av_pe] * 3
        st["attnT_pe"] = av_pe
        for q2 in range(2):
            for dh in range(2):
                qti = 2 * qg + q2
                P.wait("scalar", sPE, av_pe)
                P.wait("scalar", sPE, st["xlnT_pe_done"])
                ev = act("copy",
                         xlnT[:, qti * 1024 + dh * 512: qti * 1024 + (dh + 1) * 512],
                         psf[q2 * 2 + dh][:, :])
                st["psf_free_ac"][q2 * 2 + dh] = ev

    phases = []
    if n_sub == 1:
        phases = [("ffn", xin, yout)]
    elif n_sub == 2:
        phases = [("ffn", xin, r1), ("attn", r1, yout)]
    else:
        phases = [("ffn", xin, r1), ("attn", r1, r2), ("ffn", r2, yout)]

    for i, (kind, src, dst) in enumerate(phases):
        if i > 0:
            for eng in ("gpsimd", "sync"):
                P.wait(eng, semSt[0], semSt[0].n)
                P.wait(eng, semSt[1], semSt[1].n)
        if kind == "ffn":
            ffn_phase(src, dst, dump=(dbg and i == 0))
        else:
            attn_phase(src, dst)

    P.wait("gpsimd", semSt[0], semSt[0].n)
    P.wait("gpsimd", semSt[1], semSt[1].n)

    with nc.Block() as block:
        @block.sync
        def _(e):
            for f in P.q["sync"]:
                f(e)

        @block.gpsimd
        def _(e):
            for f in P.q["gpsimd"]:
                f(e)

        @block.vector
        def _(e):
            for f in P.q["vector"]:
                f(e)

        @block.scalar
        def _(e):
            for f in P.q["scalar"]:
                f(e)

        @block.tensor
        def _(e):
            for f in P.q["tensor"]:
                f(e)

    es.close()
    return nc


_BF = ml_dtypes.bfloat16


def _prep_weights(L0_attn_Wqkv, L0_attn_bqkv, L0_attn_Wo, L0_attn_bo,
                  L0_ffn_gate_wq, L0_ffn_up_wq, L0_ffn_down_wq,
                  L0_ffn_gate_scale, L0_ffn_up_scale, L0_ffn_down_scale):
    def qk_tiles(wq):
        A = np.asarray(wq).reshape(ITN, 128, HB, 128)
        A = np.ascontiguousarray(A.transpose(0, 3, 2, 1))
        return A.reshape(ITN, 128, HB * 128).astype(_BF)

    wg_h = qk_tiles(L0_ffn_gate_wq)
    wu_h = qk_tiles(L0_ffn_up_wq)
    Dq = np.asarray(L0_ffn_down_wq).reshape(HCN, 512, ITN, 128)
    Dq = np.ascontiguousarray(Dq.transpose(0, 2, 3, 1))
    wd_h = Dq.reshape(HCN, 43, 2, 128, 512).transpose(0, 1, 3, 2, 4)
    wd_h = np.ascontiguousarray(wd_h).reshape(HCN, 43, 128, 1024).astype(_BF)

    Wqkv = np.asarray(L0_attn_Wqkv, dtype=np.float32)
    A = Wqkv.reshape(HB, 128, 24, 128)
    wqk_h = np.ascontiguousarray(A[:, :, :16].transpose(2, 1, 0, 3)).reshape(16, 128, HB * 128)
    wv_h = np.ascontiguousarray(Wqkv[:, 2048:].reshape(HB, 128, D))
    Wo = np.asarray(L0_attn_Wo, dtype=np.float32)
    W = Wo.reshape(DT, 128, HCN, 512)
    wo_h = np.ascontiguousarray(W.transpose(2, 1, 0, 3)).reshape(HCN, 128, DT * 512)

    gs_h = np.ascontiguousarray(np.asarray(L0_ffn_gate_scale, np.float32).reshape(ITN, 128).T)
    us_h = np.ascontiguousarray(np.asarray(L0_ffn_up_scale, np.float32).reshape(ITN, 128).T)
    bq = np.asarray(L0_attn_bqkv, np.float32)
    bqkv_h = np.ascontiguousarray(bq.reshape(24, 128).T)
    bqv_h = np.ascontiguousarray(bq[2048:].reshape(1, D))
    bo_h = np.ascontiguousarray(np.asarray(L0_attn_bo, np.float32).reshape(1, H))
    ds_h = np.ascontiguousarray(np.asarray(L0_ffn_down_scale, np.float32).reshape(1, H))
    id_h = np.eye(128, dtype=_BF)
    return dict(wg_d=wg_h, wu_d=wu_h, wd_d=wd_h, wqk_d=wqk_h, wv_d=wv_h, wo_d=wo_h,
                gs_d=gs_h, us_d=us_h, bqkv_d=bqkv_h, bqv_d=bqv_h, bo_d=bo_h,
                ds_d=ds_h, id_d=id_h)


def kernel(L0_attn_x, L0_attn_Wqkv, L0_attn_bqkv, L0_attn_Wo, L0_attn_bo,
           L0_ffn_gate_wq, L0_ffn_up_wq, L0_ffn_down_wq,
           L0_ffn_gate_scale, L0_ffn_up_scale, L0_ffn_down_scale):
    x = np.asarray(L0_attn_x, dtype=np.float32)
    w = _prep_weights(L0_attn_Wqkv, L0_attn_bqkv, L0_attn_Wo, L0_attn_bo,
                      L0_ffn_gate_wq, L0_ffn_up_wq, L0_ffn_down_wq,
                      L0_ffn_gate_scale, L0_ffn_up_scale, L0_ffn_down_scale)
    nc = build_nc(n_sub=3)
    in_maps = []
    for c in range(NCORES):
        b, g = c // 4, c % 4
        m = dict(w)
        m["xin"] = np.ascontiguousarray(x[b, g * T:(g + 1) * T, :])
        in_maps.append(m)
    res = run_bass_kernel_spmd(nc, in_maps, core_ids=list(range(NCORES)))
    out = np.empty((B, S, H), dtype=np.float32)
    for c in range(NCORES):
        b, g = c // 4, c % 4
        out[b, g * T:(g + 1) * T, :] = np.asarray(res.results[c]["yout"])
    return out
